# version 29
# baseline (speedup 1.0000x reference)
import numpy as np

import concourse.bass as bass
import concourse.mybir as mybir
from concourse.bass_utils import run_bass_kernel_spmd

F32 = mybir.dt.float32
BF16 = mybir.dt.bfloat16
Act = mybir.ActivationFunctionType
Alu = mybir.AluOpType
AX = mybir.AxisListType

T, D, H, DH = 4096, 1024, 16, 64
CH = 512
NCH = T // CH
NTT = CH // 128
NJC = T // 128
NG = NJC // 2
SEG = 516
EPS = 1e-5
NHERON = 8


class Sem:
    def __init__(self, cm):
        self._cm = cm
        self.h = cm.__enter__()
        self.n = 0

    def inc(self, instr, amount=1):
        instr.then_inc(self.h, amount)
        self.n += amount
        return self.n


def build_nc():
    nc = bass.Bass("TRN2", target_bir_lowering=False, debug=False)

    x_d = nc.dram_tensor("x", [T, D], F32, kind="ExternalInput").ap()
    wq_d = nc.dram_tensor("wq", [D, D], F32, kind="ExternalInput").ap()
    wk_d = nc.dram_tensor("wk", [D, D], F32, kind="ExternalInput").ap()
    wv_d = nc.dram_tensor("wv", [D, D], F32, kind="ExternalInput").ap()
    wo_d = nc.dram_tensor("wo", [D, D], F32, kind="ExternalInput").ap()
    bo_d = nc.dram_tensor("bo", [1, D], F32, kind="ExternalInput").ap()
    wr_d = nc.dram_tensor("wr", [D, D], F32, kind="ExternalInput").ap()
    wk2_d = nc.dram_tensor("wk2", [D, D], F32, kind="ExternalInput").ap()
    wv2_d = nc.dram_tensor("wv2", [D, D], F32, kind="ExternalInput").ap()
    mur_d = nc.dram_tensor("mur", [128, 8], F32, kind="ExternalInput").ap()
    muk_d = nc.dram_tensor("muk", [128, 8], F32, kind="ExternalInput").ap()
    y_d = nc.dram_tensor("y", [T, D], F32, kind="ExternalOutput").ap()
    ks_d = nc.dram_tensor("ks", [D, T], BF16, kind="Internal").ap()
    vs_d = nc.dram_tensor("vs", [H, 128, NJC * DH], BF16, kind="Internal").ap()
    xns_d = nc.dram_tensor("xns", [128, 8 * T], BF16, kind="Internal").ap()

    A = nc.alloc_sbuf_tensor
    wq_sb = A("wq_sb", [128, 8192], BF16).ap()
    wo_sb = A("wo_sb", [128, 8192], BF16).ap()
    wr_t = A("wr_t", [128, 8192], BF16).ap()
    wk2_t = A("wk2_t", [128, 8192], BF16).ap()
    wv2_t = A("wv2_t", [128, 8192], BF16).ap()
    regA = A("regA", [128, 32768], mybir.dt.uint8).ap()
    regA_bf = regA.bitcast(BF16)
    regA_f32 = regA.bitcast(F32)
    wkp_sb = regA_bf[:, 0:8192]
    wvp_sb = regA_bf[:, 8192:16384]
    xr_sb = regA_bf[:, 0:4096]
    xk_sb = regA_bf[:, 4096:8192]
    kact_sb = regA_bf[:, 8192:12288]
    tanh_f = regA_f32[:, 6144:7168]
    tmp_f = regA_f32[:, 7168:8192]

    x_sb = A("x_sb", [128, 4096], F32).ap()
    qz = x_sb.bitcast(BF16)
    xb = [qz[:, 0:4096], qz[:, 4096:8192]]
    xn_fm = A("xn_fm", [128, 8 * SEG], BF16).ap()
    q_sb = A("q_sb", [128, 4096], BF16).ap()
    kh0 = A("kh0", [128, 4096], BF16).ap()
    kh1 = A("kh1", [128, 4096], BF16).ap()
    vhe = A("vhe", [128, 2080], BF16).ap()
    vho = A("vho", [128, 4096], BF16).ap()
    p_sb = A("p_sb", [128, 4096], BF16).ap()
    ofm = A("ofm", [128, 4096], BF16).ap()
    wtmp = ofm[:, 0:1024]
    of32 = A("of32", [128, 512], F32).ap()
    rl_sb = A("rl_sb", [128, 512], BF16).ap()
    xn_tm = of32.bitcast(BF16)[:, 0:1024]
    q2_sb = rl_sb
    out_sb = A("out_sb", [128, 2048], F32).ap()
    wld0 = out_sb[:, 0:1024]
    wld1 = out_sb[:, 1024:2048]
    xres = A("xres", [128, 1024], F32).ap()
    k_sb = kh0[:, 0:1024]
    v_sb = kh0[:, 1024:2048]
    k2_sb = kh0[:, 2048:3584]
    ident = A("ident", [128, 128], BF16).ap()
    ones_sb = A("ones_sb", [128, 128], BF16).ap()
    selp_sb = A("selp_sb", [128, 2], BF16).ap()
    mur_sb = A("mur_sb", [128, 8], F32).ap()
    muk_sb = A("muk_sb", [128, 8], F32).ap()
    bo_bf = A("bo_bf", [1, 1024], BF16).ap()
    negb = A("negb", [128, 16], F32).ap()
    mq2r = A("mq2r", [1, 16], F32).ap()
    mk2r = A("mk2r", [1, 16], F32).ap()
    nbr = A("nbr", [1, 16], BF16).ap()
    tr1 = A("tr1", [1, 16], F32).ap()
    stat = A("stat", [128, 32], F32).ap()
    m_c, ex_c = stat[:, 0:4], stat[:, 4:8]
    vv_c, vh_c = stat[:, 8:12], stat[:, 12:16]
    s_c, r_c = stat[:, 16:20], stat[:, 20:24]
    t1_c, rstd_c = stat[:, 24:28], stat[:, 28:32]

    pA = nc.alloc_psum_tensor("pA", [128, 1024], F32).ap()
    pB = nc.alloc_psum_tensor("pB", [128, 1024], F32).ap()
    pO = nc.alloc_psum_tensor("pO", [128, 1024], F32).ap()
    pC = nc.alloc_psum_tensor("pC", [128, 1024], F32).ap()
    pE = pC[:, 0:512]
    pTs = [pC.bitcast(BF16)[:, 0:1024],
           pC.bitcast(BF16)[:, 1024:2048]]
    pT = pTs[1]

    sems = []

    def mk(name):
        s = Sem(nc.semaphore(name))
        sems.append(s)
        return s

    sW = mk("sW")
    sX = mk("sX")
    sXR = mk("sXR")
    sXN = mk("sXN")
    sKH = mk("sKH")
    sVH = mk("sVH")
    sSP = mk("sSP")
    sOUT = mk("sOUT")
    sT = mk("sT")
    sS = mk("sS")
    sPV = mk("sPV")
    sE = mk("sE")
    sV = mk("sV")
    sSC = mk("sSC")
    sRL = mk("sRL")
    sRLB = mk("sRLB")
    sOEV = mk("sOEV")
    sG = mk("sG")

    te, ve, se, sy, gp = nc.tensor, nc.vector, nc.scalar, nc.sync, nc.gpsimd

    xn3 = xn_fm.rearrange("p (s x) -> p s x", x=SEG)
    pT3 = pT.rearrange("p (s x) -> p s x", x=128)

    with nc.Block() as block:
        sW.inc(sy.dma_start(out=mur_sb[:, :], in_=mur_d[:, :]), 16)
        sW.inc(sy.dma_start(out=muk_sb[:, :], in_=muk_d[:, :]), 16)
        sW.inc(sy.dma_start(out=wld1[0:1, :], in_=bo_d[0:1, :]), 16)
        ve.wait_ge(sW.h, sW.n)
        ve.tensor_copy(bo_bf[0:1, :], wld1[0:1, :])
        ve.memset(mk2r[0:1, :], 0.0)
        sV.inc(ve.memset(xn3[:, :, 0:1], 0.0))

        gp.memset(ident[:, :], 0.0)
        gp.drain()
        gp.affine_select(out=ident[:, :], in_=ident[:, :], compare_op=Alu.not_equal,
                         fill=1.0, base=0, pattern=[[-1, 128]], channel_multiplier=1)
        gp.memset(ones_sb[:, :], 1.0)
        gp.drain()
        gp.memset(selp_sb[:, :], 0.0)
        gp.drain()
        gp.memset(selp_sb[0:64, 0:1], 1.0)
        ident_ready = sG.inc(gp.memset(selp_sb[64:128, 1:2], 1.0))
        gp.memset(vhe[:, :], 0.0)
        gp.memset(vho[:, :], 0.0)
        vhe3 = vhe.rearrange("p (jc c) -> p jc c", c=65)
        vho3 = vho.rearrange("p (jc c) -> p jc c", c=128)
        gp.memset(vhe3[:, :, 64:65], 1.0)
        sG.inc(gp.memset(vho3[:, :, 32:33], 1.0))
        vh_ready = sG.n

        for wd, wsb in [(wq_d, wq_sb), (wo_d, wo_sb), (wk_d, wkp_sb), (wv_d, wvp_sb)]:
            sW.inc(gp.dma_start(
                out=wsb.rearrange("p (c x) -> p c x", x=1024),
                in_=wd.rearrange("(c p) x -> p c x", p=128)), 16)
        ve.wait_ge(sW.h, sW.n)
        te.wait_ge(sG.h, ident_ready)
        for wd, wsb, scale in [(wr_d, wr_t, None), (wk2_d, wk2_t, None), (wv2_d, wv2_t, None)]:
            wsb3 = wsb.rearrange("p (s x) -> p s x", x=1024)
            for dc in range(8):
                wslot = wtmp if dc % 2 == 0 else ofm[:, 1024:2048]
                if scale is None:
                    gp.wait_ge(sT.h, sT.n)
                    sW.inc(gp.dma_start(out=wslot,
                                        in_=wd[dc * 128:(dc + 1) * 128, :]), 16)
                    te.wait_ge(sW.h, sW.n)
                else:
                    slot = wld0 if dc % 2 == 0 else wld1
                    sy.wait_ge(sV.h, sV.n)
                    sW.inc(sy.dma_start(out=slot[:, :],
                                        in_=wd[dc * 128:(dc + 1) * 128, :]), 16)
                    ve.wait_ge(sW.h, sW.n)
                    sV.inc(ve.tensor_scalar_mul(wslot, slot[:, :], scale))
                    te.wait_ge(sV.h, sV.n)
                pts = pTs[dc % 2]
                if dc >= 2:
                    te.wait_ge(sV.h, sV.n)
                for ct in range(8):
                    ti = te.transpose(pts[:, ct * 128:(ct + 1) * 128],
                                      wslot[:, ct * 128:(ct + 1) * 128], ident[:, :])
                sT.inc(ti)
                ve.wait_ge(sT.h, sT.n)
                sV.inc(ve.tensor_copy(wsb3[:, :, dc * 128:dc * 128 + 128],
                                      pts.rearrange("p (s x) -> p s x", x=128)))

        spill_mark = [0]
        transp_done = {}
        evac_done = {}

        def emit_ln_xn(x_wait_n, sink, slot):
            ve.wait_ge(sX.h, x_wait_n)
            for tt in range(NTT):
                xt = xb[slot][:, tt * 1024:(tt + 1) * 1024]
                ve.reduce_sum(m_c[:, tt:tt + 1], xt, axis=AX.X)
                ve.tensor_tensor(sink, xt, xt, op=Alu.mult)
                ve.drain()
                ve.reduce_sum(ex_c[:, tt:tt + 1], sink, axis=AX.X)
            ve.drain()
            ve.tensor_scalar_mul(ex_c[:, :], ex_c[:, :], 1.0 / 1024.0)
            ve.tensor_scalar_mul(m_c[:, :], m_c[:, :], 1.0 / 1024.0)
            ve.drain()
            ve.tensor_tensor(t1_c[:, :], m_c[:, :], m_c[:, :], op=Alu.mult)
            ve.drain()
            ve.tensor_tensor(vv_c[:, :], ex_c[:, :], t1_c[:, :], op=Alu.subtract)
            ve.drain()
            ve.tensor_scalar_add(vv_c[:, :], vv_c[:, :], EPS)
            ve.drain()
            ve.tensor_scalar_mul(vh_c[:, :], vv_c[:, :], 0.5)
            ve.tensor_scalar(s_c[:, :], vv_c[:, :], 0.5, 0.5, op0=Alu.mult, op1=Alu.add)
            ve.drain()
            for _ in range(NHERON):
                ve.reciprocal(r_c[:, :], s_c[:, :])
                ve.drain()
                ve.tensor_tensor(t1_c[:, :], vh_c[:, :], r_c[:, :], op=Alu.mult)
                ve.drain()
                ve.scalar_tensor_tensor(s_c[:, :], s_c[:, :], 0.5, t1_c[:, :],
                                        op0=Alu.mult, op1=Alu.add)
                ve.drain()
            ve.reciprocal(rstd_c[:, :], s_c[:, :])
            ve.drain()
            last_read = None
            for tt in range(NTT):
                xt = xb[slot][:, tt * 1024:(tt + 1) * 1024]
                xnt = q_sb[:, (tt % 2) * 1024:(tt % 2) * 1024 + 1024]
                if tt > 1:
                    ve.wait_ge(sT.h, transp_done[tt - 2])
                i = ve.tensor_scalar(xnt, xt, m_c[:, tt:tt + 1],
                                     rstd_c[:, tt:tt + 1],
                                     op0=Alu.subtract, op1=Alu.mult)
                last_read = sV.inc(i)
                te.wait_ge(sV.h, sV.n)
                if tt > 1:
                    te.wait_ge(sSC.h, evac_done[tt - 2])
                pts = pTs[tt % 2]
                for c in range(8):
                    ti = te.transpose(pts[:, c * 128:(c + 1) * 128],
                                      xnt[:, c * 128:(c + 1) * 128], ident[:, :])
                transp_done[tt] = sT.inc(ti)
                se.wait_ge(sT.h, transp_done[tt])
                if tt == 0:
                    se.wait_ge(sSP.h, spill_mark[0])
                evac_done[tt] = sSC.inc(
                    se.activation(xn3[:, :, 1 + tt * 128: 1 + tt * 128 + 128],
                                  pts.rearrange("p (s x) -> p s x", x=128),
                                  Act.Copy))
            return last_read

        x_wait = []
        xread_done = {}
        gp.wait_ge(sG.h, sG.n)
        sX.inc(gp.dma_start(out=xb[0].rearrange("p (tt c) -> p tt c", c=1024),
                            in_=x_d[0:512, :].rearrange("(tt p) c -> p tt c", p=128)), 16)
        x_wait.append(sX.n)
        sX.inc(gp.dma_start(out=xb[1].rearrange("p (tt c) -> p tt c", c=1024),
                            in_=x_d[512:1024, :].rearrange("(tt p) c -> p tt c", p=128)), 16)
        x_wait.append(sX.n)

        for ch in range(NCH):
            last_x_read = emit_ln_xn(x_wait[ch], out_sb[:, 0:1024], ch % 2)
            xread_done[ch] = last_x_read
            if ch + 2 < NCH:
                gp.wait_ge(sV.h, xread_done[ch])
                base = (ch + 2) * CH
                sX.inc(gp.dma_start(
                    out=xb[ch % 2].rearrange("p (tt c) -> p tt c", c=1024),
                    in_=x_d[base:base + 512, :].rearrange("(tt p) c -> p tt c", p=128)), 16)
                x_wait.append(sX.n)

            gp.wait_ge(sSC.h, sSC.n)
            xns3 = xns_d.rearrange("p (s t) -> p s t", t=T)
            sSP.inc(gp.dma_start(out=xns3[:, :, ch * 512:(ch + 1) * 512],
                                 in_=xn3[:, :, 1:513]), 16)
            spill_mark[0] = sSP.n

            kev = {}
            kspill = {}
            bnd_done = {}
            rmx_done = {}

            def emit_b_bounds(dp):
                for par in range(2):
                    h = dp * 2 + par
                    te.wait_ge(sSC.h, kev[dp])
                    if h >= 2:
                        te.wait_ge(sV.h, rmx_done[h - 2])
                    pes = pO[:, (h % 2) * 512:(h % 2) * 512 + 512]
                    bm = te.matmul(pes[0:1, :], selp_sb[:, par:par + 1],
                                   k2_sb[:, (dp % 3) * 512:(dp % 3) * 512 + 512],
                                   start=True, stop=True)
                    bm_done = sT.inc(bm)
                    ve.wait_ge(sT.h, bm_done)
                    ve.reduce_max(tr1[0:1, 0:1], pes[0:1, :], axis=AX.X)
                    ve.drain()
                    rmx_done[h] = sV.inc(
                        ve.tensor_tensor(mk2r[0:1, h:h + 1], mk2r[0:1, h:h + 1],
                                         tr1[0:1, 0:1], op=Alu.max))
                    ve.drain()
                bnd_done[dp] = sT.n

            for dp in range(8):
                half = pA[:, 0:512] if dp % 2 == 0 else pA[:, 512:1024]
                if dp == 0:
                    te.wait_ge(sSC.h, sSC.n)
                if dp >= 2:
                    te.wait_ge(sSC.h, kev[dp - 2])
                for c in range(8):
                    mi = te.matmul(half,
                                   wkp_sb[:, c * 1024 + dp * 128: c * 1024 + dp * 128 + 128],
                                   xn3[:, c, 1:513],
                                   start=(c == 0), stop=(c == 7))
                kp_done = sT.inc(mi)
                if dp >= 2:
                    emit_b_bounds(dp - 2)
                kslot = k_sb[:, (dp % 2) * 512:(dp % 2) * 512 + 512]
                se.wait_ge(sT.h, kp_done)
                if dp >= 2:
                    se.wait_ge(sSP.h, kspill[dp - 2])
                if dp >= 3:
                    se.wait_ge(sT.h, bnd_done[dp - 3])
                sSC.inc(se.activation(kslot, half, Act.Copy))
                kev[dp] = sSC.inc(se.activation(
                    k2_sb[:, (dp % 3) * 512:(dp % 3) * 512 + 512], kslot, Act.Square))
                gp.wait_ge(sSC.h, kev[dp])
                sSP.inc(gp.dma_start(
                    out=ks_d[dp * 128:(dp + 1) * 128, ch * 512:(ch + 1) * 512],
                    in_=kslot), 16)
                kspill[dp] = sSP.n
            emit_b_bounds(6)
            emit_b_bounds(7)

            for tt in range(NTT):
                g = ch * NTT + tt
                te.wait_ge(sSC.h, sSC.n)
                for c in range(8):
                    for nh in range(2):
                        mi = te.matmul(pB[:, nh * 512:(nh + 1) * 512],
                                       xn3[:, c, 1 + tt * 128: 1 + tt * 128 + 128],
                                       wvp_sb[:, c * 1024 + nh * 512: c * 1024 + (nh + 1) * 512],
                                       start=(c == 0), stop=(c == 7))
                vp_done = sT.inc(mi)
                se.wait_ge(sT.h, vp_done)
                se.wait_ge(sSP.h, sSP.n)
                vev = sSC.inc(se.activation(v_sb[:, :], pB[:, :], Act.Copy))
                gp.wait_ge(sSC.h, vev)
                sSP.inc(gp.dma_start(
                    out=vs_d[:, :, g * 64:(g + 1) * 64].transpose([1, 0, 2]),
                    in_=v_sb.rearrange("j (h c) -> j h c", c=64)), 16)

        spill_total = sSP.n

        xns3 = xns_d.rearrange("p (s t) -> p s t", t=T)
        sy.wait_ge(sSP.h, sSP.n)
        sy.wait_ge(sV.h, sV.n)
        sXN.inc(sy.dma_start(out=xn3[:, :, 1:513], in_=xns3[:, :, 0:512]), 16)
        xn_wait = [sXN.n]
        ve.memset(qz[:, :], 0.0)
        ve.drain()
        sXR.inc(sy.dma_start(out=xres[:, :], in_=x_d[0:128, :]), 16)
        xres_wait = sXR.n

        kh_bufs = [kh0, kh1]
        kh_wait = {}
        s_pair_end = {}

        def load_kh(pg):
            p = pg % 8
            if pg >= 2:
                sy.wait_ge(sS.h, s_pair_end[pg - 2])
            sKH.inc(sy.dma_start(out=kh_bufs[pg % 2][:, :],
                                 in_=ks_d[p * 128:(p + 1) * 128, :]), 16)
            kh_wait[pg] = sKH.n

        vh_wait = {}
        pv_end = {}

        def load_vh(hg):
            h = hg % H
            if h % 2 == 0:
                dst = vhe.rearrange("p (jc c) -> p jc c", c=65)[:, :, 0:64]
            else:
                dst = vho.rearrange("p (jc c) -> p jc c", c=128)[:, :, 64:128]
            src = vs_d[h].rearrange("j (jc c) -> j jc c", c=64)
            if hg >= 2:
                gp.wait_ge(sPV.h, pv_end[hg - 2])
            sVH.inc(gp.dma_start(out=dst[:, 0:16, :], in_=src[:, 0:16, :]), 16)
            sVH.inc(gp.dma_start(out=dst[:, 16:32, :], in_=src[:, 16:32, :]), 16)
            vh_wait[hg] = sVH.n

        sy.wait_ge(sSP.h, spill_total)
        gp.wait_ge(sSP.h, spill_total)
        gp.wait_ge(sG.h, vh_ready)
        ve.wait_ge(sT.h, sT.n)
        load_kh(0)
        load_vh(0)
        load_vh(1)

        evac_end = {}
        recip_rdy = {}
        rank1_rdy = {}
        negb_cell = [0]

        def emit_rank1(hg):
            te.wait_ge(sRL.h, recip_rdy[hg])
            lr = 64 if hg % 2 == 0 else 32
            dst = pO[:, (hg % 2) * 512:(hg % 2) * 512 + 512]
            r1 = te.matmul(dst, ones_sb[lr:lr + 1, 0:128],
                           rl_sb[lr:lr + 1, 0:512], start=True, stop=True)
            rank1_rdy[hg] = sRLB.inc(r1)

        def emit_tt_norm(hg):
            h = hg % H
            p = h // 2
            orows = slice(0, 64) if h % 2 == 0 else slice(64, 128)
            src = pO[:, (hg % 2) * 512:(hg % 2) * 512 + 512]
            ve.wait_ge(sRLB.h, rank1_rdy[hg])
            i = ve.tensor_tensor(ofm[orows, p * 512:(p + 1) * 512],
                                 of32[orows, :], src[orows, :], op=Alu.mult)
            evac_end[hg] = sOEV.inc(i)

        ffn_prep_prev = 0
        tensor_ffn_prev = 0
        q_evac_nx = {}
        sq_done_nx = {}
        qp_done_nx = {}
        for ch in range(NCH):
            if ch > 0:
                te.wait_ge(sOEV.h, sOEV.n)
                te.wait_ge(sSC.h, sSC.n)
            te.wait_ge(sXN.h, xn_wait[ch])
            q_evac = {}
            sq_done = {}
            qp_done = {}
            bounds_done = {}
            redmax_done = {}

            def emit_qmm(dp):
                half = pA[:, 0:512] if dp % 2 == 0 else pA[:, 512:1024]
                if dp >= 2:
                    te.wait_ge(sV.h, q_evac[dp - 2])
                if dp >= 4:
                    se.wait_ge(sT.h, bounds_done[dp - 4])
                for c in range(8):
                    mi = te.matmul(half,
                                   wq_sb[:, c * 1024 + dp * 128: c * 1024 + dp * 128 + 128],
                                   xn3[:, c, 1:513],
                                   start=(c == 0), stop=(c == 7))
                qp_done[dp] = sT.inc(mi)
                ve.wait_ge(sT.h, qp_done[dp])
                h0, h1 = 2 * dp, 2 * dp + 1
                ve.tensor_copy(qz[0:64, h0 * 512:(h0 + 1) * 512], half[0:64, :])
                q_evac[dp] = sV.inc(
                    ve.tensor_copy(qz[64:128, h1 * 512:(h1 + 1) * 512], half[64:128, :]))
                se.wait_ge(sV.h, q_evac[dp])
                i0 = se.activation(p_sb[:, (h0 % 8) * 512:(h0 % 8) * 512 + 512],
                                   qz[:, h0 * 512:(h0 + 1) * 512], Act.Square)
                sSC.inc(i0)
                sq_done[dp] = sSC.inc(
                    se.activation(p_sb[:, (h1 % 8) * 512:(h1 % 8) * 512 + 512],
                                  qz[:, h1 * 512:(h1 + 1) * 512], Act.Square))

            def emit_bounds(dp):
                for par in range(2):
                    h = dp * 2 + par
                    rows = slice(0, 64) if par == 0 else slice(64, 128)
                    te.wait_ge(sSC.h, sq_done[dp])
                    if dp >= 1 or par == 1:
                        te.wait_ge(sV.h, redmax_done[h - 2] if h >= 2 else 0)
                    q2slot = p_sb[:, (h % 8) * 512:(h % 8) * 512 + 512]
                    pes = pC[:, (h % 2) * 512:(h % 2) * 512 + 512]
                    bm = te.matmul(pes[0:1, :], ones_sb[rows, 0:1],
                                   q2slot[rows, :], start=True, stop=True)
                    bm_done = sT.inc(bm)
                    ve.wait_ge(sT.h, bm_done)
                    redmax_done[h] = sV.inc(
                        ve.reduce_max(mq2r[0:1, h:h + 1], pes[0:1, :], axis=AX.X))
                    ve.drain()
                bounds_done[dp] = sT.n

            for dp in range(4):
                emit_qmm(dp)
            for dp in range(4, 8):
                emit_bounds(dp - 4)
                emit_qmm(dp)
            for dp in range(4, 8):
                emit_bounds(dp)
            if True:
                    pass
            ve.drain()
            ve.tensor_tensor(tr1[0:1, :], mq2r[0:1, :], mk2r[0:1, :], op=Alu.add)
            ve.drain()
            nb_done = sV.inc(ve.tensor_scalar_mul(nbr[0:1, :], tr1[0:1, :], -1.0 / 16.0))
            ve.drain()
            te.wait_ge(sV.h, nb_done)
            nb_mm_done = sT.inc(te.matmul(pE[:, 0:16], ones_sb[0:1, 0:128],
                                          nbr[0:1, 0:16], start=True, stop=True))
            ve.wait_ge(sT.h, nb_mm_done)
            negb_done = sV.inc(ve.tensor_copy(negb[:, :], pE[:, 0:16]))
            negb_cell[0] = negb_done

            ve.wait_ge(sT.h, tensor_ffn_prev)
            last = None
            for sg in range(8):
                dseg = kact_sb[:, sg * 512:(sg + 1) * 512]
                ve.tensor_tensor(dseg, xn3[:, sg, 1:513], xn3[:, sg, 0:512],
                                 op=Alu.subtract)
                ve.drain()
                ve.scalar_tensor_tensor(xr_sb[:, sg * 512:(sg + 1) * 512],
                                        dseg, mur_sb[:, sg:sg + 1],
                                        xn3[:, sg, 0:512],
                                        op0=Alu.mult, op1=Alu.add)
                last = ve.scalar_tensor_tensor(xk_sb[:, sg * 512:(sg + 1) * 512],
                                               dseg, muk_sb[:, sg:sg + 1],
                                               xn3[:, sg, 0:512],
                                               op0=Alu.mult, op1=Alu.add)
            ffn_prep = sV.inc(last)

            sc_mark = sSC.n
            NGC = H * NG
            sregs = [pA, pB, pC]
            s_done = {}
            exp_done = {}
            pv_done = {}

            def head_entry_waits(h):
                hg = ch * H + h
                pg = ch * 8 + h // 2
                if h % 2 == 0 and pg + 1 < NCH * 8:
                    load_kh(pg + 1)
                te.wait_ge(sV.h, q_evac[7] if h == 0 else q_evac[h // 2])
                te.wait_ge(sKH.h, kh_wait[pg])
                te.wait_ge(sVH.h, vh_wait[hg])

            def emit_S(g):
                h = g // NG
                lg = g % NG
                if lg == 0:
                    head_entry_waits(h)
                khb = kh_bufs[(ch * 8 + h // 2) % 2]
                reg = sregs[g % 3]
                for half in range(2):
                    jc = 2 * lg + half
                    mi = te.matmul(reg[:, half * 512:(half + 1) * 512],
                                   khb[:, jc * 128:(jc + 1) * 128],
                                   qz[:, h * 512:(h + 1) * 512],
                                   start=True, stop=True)
                s_done[g] = sS.inc(mi)
                if lg == NG - 1 and h % 2 == 1:
                    s_pair_end[ch * 8 + h // 2] = sS.n

            def emit_exp(g):
                h = g // NG
                reg = sregs[g % 3]
                se.wait_ge(sS.h, s_done[g])
                if g == 0:
                    se.wait_ge(sV.h, negb_done)
                if g >= 4:
                    se.wait_ge(sPV.h, pv_done[g - 4])
                slot = p_sb[:, (g % 4) * 1024:(g % 4 + 1) * 1024]
                ei = se.activation(slot, reg[:, :], Act.Exp,
                                   bias=negb[:, h:h + 1], scale=0.125)
                exp_done[g] = sE.inc(ei)

            def emit_PV(g):
                h = g // NG
                hg = ch * H + h
                lg = g % NG
                if lg == 0 and hg >= 2:
                    te.wait_ge(sOEV.h, evac_end[hg - 2])
                vhb = vhe if h % 2 == 0 else vho
                ohalf = pO[:, 0:512] if h % 2 == 0 else pO[:, 512:1024]
                slot = p_sb[:, (g % 4) * 1024:(g % 4 + 1) * 1024]
                for half in range(2):
                    jc = 2 * lg + half
                    if h % 2 == 0:
                        mi = te.matmul(ohalf[0:65, :],
                                       vhb[:, jc * 65:(jc + 1) * 65],
                                       slot[:, half * 512:(half + 1) * 512],
                                       start=(jc == 0), stop=(jc == NJC - 1))
                    else:
                        mi = te.matmul(ohalf,
                                       vhb[:, jc * 128:(jc + 1) * 128],
                                       slot[:, half * 512:(half + 1) * 512],
                                       start=(jc == 0), stop=(jc == NJC - 1))
                pv_done[g] = sPV.inc(mi)

            def emit_head_epilogue(h):
                hg = ch * H + h
                pv_end[hg] = pv_done[(h + 1) * NG - 1]
                if hg + 2 < NCH * H:
                    load_vh(hg + 2)
                if hg >= 1 and (hg - 1) not in evac_end:
                    emit_tt_norm(hg - 1)
                rows = slice(0, 64) if h % 2 == 0 else slice(64, 128)
                ohalf = pO[:, 0:512] if h % 2 == 0 else pO[:, 512:1024]
                lrow = 64 if h % 2 == 0 else 32
                ve.wait_ge(sPV.h, pv_end[hg])
                if h == 0:
                    ve.wait_ge(sSC.h, sc_mark)
                ve.tensor_copy(of32[rows, :], ohalf[rows, :])
                with nc.allow_low_precision(reason="1/l feeds a bf16 matmul"):
                    ve.reciprocal(rl_sb[lrow:lrow + 1, :], ohalf[lrow:lrow + 1, :])
                ve.drain()
                recip_rdy[hg] = sRL.inc(ve.engine_nop())

            for gg in range(0, NGC, 2):
                if gg >= 2:
                    te.wait_ge(sE.h, exp_done[gg - 2])
                emit_S(gg)
                emit_S(gg + 1)
                for g2 in (gg - 3, gg - 2):
                    if g2 >= 0:
                        emit_PV(g2)
                        if g2 % NG == NG - 1:
                            emit_head_epilogue(g2 // NG)
                emit_exp(gg)
                emit_exp(gg + 1)
                for sub in (gg, gg + 1):
                    if sub % NG == 9:
                        hprev = ch * H + sub // NG - 1
                        if hprev >= 0 and hprev not in rank1_rdy:
                            emit_rank1(hprev)
            te.wait_ge(sE.h, exp_done[NGC - 1])
            for g2 in (NGC - 3, NGC - 2, NGC - 1):
                emit_PV(g2)
                if g2 % NG == NG - 1:
                    emit_head_epilogue(g2 // NG)

            emit_rank1(ch * H + H - 1)
            emit_tt_norm(ch * H + H - 1)
            all_evac = evac_end[ch * H + H - 1]

            te.wait_ge(sOEV.h, all_evac)
            te.wait_ge(sV.h, ffn_prep)
            kact_evac = {}
            for cp in range(8):
                half = pA[:, 0:512] if cp % 2 == 0 else pA[:, 512:1024]
                if cp >= 2:
                    te.wait_ge(sSC.h, kact_evac[cp - 2])
                for c in range(8):
                    mi = te.matmul(half,
                                   wk2_t[:, c * 1024 + cp * 128: c * 1024 + cp * 128 + 128],
                                   xk_sb[:, c * 512:(c + 1) * 512],
                                   start=(c == 0), stop=(c == 7))
                km_done = sT.inc(mi)
                ve.wait_ge(sT.h, km_done)
                if cp >= 1:
                    ve.wait_ge(sSC.h, kact_evac[cp - 1])
                relu_done = sV.inc(ve.tensor_scalar_max(of32[:, :], half, 0.0))
                se.wait_ge(sV.h, relu_done)
                kact_evac[cp] = sSC.inc(
                    se.activation(kact_sb[:, cp * 512:(cp + 1) * 512],
                                  of32[:, :], Act.Square))

            tanh_done = {}
            comb_stt = {}
            comb_u1 = {}
            for tt in range(NTT):
                if tt == 0:
                    te.wait_ge(sSC.h, kact_evac[7])
                else:
                    te.wait_ge(sV.h, comb_u1[tt - 1])
                for half in range(2):
                    dst = pA[:, half * 512:(half + 1) * 512]
                    for dpc in range(8):
                        te.matmul(dst,
                                  ofm[:, dpc * 512 + tt * 128: dpc * 512 + tt * 128 + 128],
                                  wo_sb[:, dpc * 1024 + half * 512: dpc * 1024 + (half + 1) * 512],
                                  start=(dpc == 0), stop=False)
                    mi = te.matmul(dst, ones_sb[0:1, 0:128],
                                   bo_bf[0:1, half * 512:(half + 1) * 512],
                                   start=False, stop=True)
                oproj_done = sT.inc(mi)
                glob_tt = ch * NTT + tt
                oslot = out_sb[:, (glob_tt % 2) * 1024:(glob_tt % 2 + 1) * 1024]
                ve.wait_ge(sT.h, oproj_done)
                if glob_tt >= 2:
                    ve.wait_ge(sOUT.h, 16 * (glob_tt - 1))
                ve.wait_ge(sXR.h, xres_wait)
                comb_u1[tt] = sV.inc(ve.tensor_tensor(oslot, xres[:, :], pA[:, :],
                                                      op=Alu.add))
                ve.drain()
                ntt = glob_tt + 1
                if ntt < NCH * NTT:
                    sy.wait_ge(sV.h, comb_u1[tt])
                    sXR.inc(sy.dma_start(out=xres[:, :],
                                         in_=x_d[ntt * 128:(ntt + 1) * 128, :]), 16)
                    xres_wait = sXR.n

                if ch + 1 < NCH and tt >= 2:
                    if tt == 2:
                        te.wait_ge(sXN.h, xn_wait[ch + 1])
                        te.wait_ge(sE.h, exp_done[NGC - 1])
                    for dp in range(4 * (tt - 2), 4 * (tt - 1)):
                        half = pC[:, (dp % 2) * 512:(dp % 2) * 512 + 512]
                        if dp >= 2:
                            te.wait_ge(sV.h, q_evac_nx[dp - 2])
                        for c in range(8):
                            mi = te.matmul(half,
                                           wq_sb[:, c * 1024 + dp * 128: c * 1024 + dp * 128 + 128],
                                           xn3[:, c, 1:513],
                                           start=(c == 0), stop=(c == 7))
                        qp_done_nx[dp] = sT.inc(mi)
                        ve.wait_ge(sT.h, qp_done_nx[dp])
                        h0, h1 = 2 * dp, 2 * dp + 1
                        ve.tensor_copy(qz[0:64, h0 * 512:(h0 + 1) * 512], half[0:64, :])
                        q_evac_nx[dp] = sV.inc(
                            ve.tensor_copy(qz[64:128, h1 * 512:(h1 + 1) * 512],
                                           half[64:128, :]))
                        se.wait_ge(sV.h, q_evac_nx[dp])
                        i0 = se.activation(p_sb[:, (h0 % 8) * 512:(h0 % 8) * 512 + 512],
                                           qz[:, h0 * 512:(h0 + 1) * 512], Act.Square)
                        sSC.inc(i0)
                        sq_done_nx[dp] = sSC.inc(
                            se.activation(p_sb[:, (h1 % 8) * 512:(h1 % 8) * 512 + 512],
                                          qz[:, h1 * 512:(h1 + 1) * 512], Act.Square))

                if tt > 0:
                    te.wait_ge(sSC.h, tanh_done[tt - 1])
                for c in range(8):
                    for half in range(2):
                        mi = te.matmul(pB[:, half * 512:(half + 1) * 512],
                                       xr_sb[:, c * 512 + tt * 128: c * 512 + tt * 128 + 128],
                                       wr_t[:, c * 1024 + half * 512: c * 1024 + (half + 1) * 512],
                                       start=(c == 0), stop=(c == 7))
                rproj_done = sT.inc(mi)

                if tt > 0:
                    te.wait_ge(sV.h, comb_stt[tt - 1])
                for cp in range(8):
                    for half in range(2):
                        mi = te.matmul(pO[:, half * 512:(half + 1) * 512],
                                       kact_sb[:, cp * 512 + tt * 128: cp * 512 + tt * 128 + 128],
                                       wv2_t[:, cp * 1024 + half * 512: cp * 1024 + (half + 1) * 512],
                                       start=(cp == 0), stop=(cp == 7))
                vproj_done = sT.inc(mi)

                se.wait_ge(sT.h, rproj_done)
                if tt > 0:
                    se.wait_ge(sV.h, comb_stt[tt - 1])
                tanh_done[tt] = sSC.inc(se.activation(tanh_f, pB[:, :],
                                                      Act.Tanh, scale=0.5))

                ve.wait_ge(sT.h, vproj_done)
                ve.wait_ge(sSC.h, tanh_done[tt])
                comb_stt[tt] = sV.inc(
                    ve.scalar_tensor_tensor(tmp_f, tanh_f, 1.0,
                                            pO[:, :], op0=Alu.add, op1=Alu.mult))
                ve.drain()
                comb_fin = sV.inc(ve.scalar_tensor_tensor(oslot, tmp_f, 0.5, oslot,
                                                          op0=Alu.mult, op1=Alu.add))
                ve.drain()

                gp.wait_ge(sV.h, comb_fin)
                base = ch * CH + tt * 128
                sOUT.inc(gp.dma_start(out=y_d[base:base + 128, :], in_=oslot), 16)
                if tt == NTT - 1:
                    tensor_ffn_prev = sT.n
                if tt == 1 and ch + 1 < NCH:
                    sy.wait_ge(sV.h, ffn_prep)
                    base = (ch + 1) * 512
                    sXN.inc(sy.dma_start(out=xn3[:, :, 0:513],
                                         in_=xns3[:, :, base - 1: base + 512]), 16)
                    xn_wait.append(sXN.n)

        gp.wait_ge(sOUT.h, sOUT.n)

    for s in sems:
        s._cm.__exit__(None, None, None)
    return nc


_NC_CACHE = None


def kernel(L0_attn_x, L0_attn_Wq, L0_attn_Wk, L0_attn_Wv, L0_attn_Wo, L0_attn_bo,
           L0_ffn_W_r, L0_ffn_W_k, L0_ffn_W_v, L0_ffn_mu_r, L0_ffn_mu_k):
    global _NC_CACHE
    x = np.ascontiguousarray(np.asarray(L0_attn_x, dtype=np.float32))
    B = x.shape[0]
    w = {
        "wq": np.ascontiguousarray(np.asarray(L0_attn_Wq, np.float32)),
        "wk": np.ascontiguousarray(np.asarray(L0_attn_Wk, np.float32)),
        "wv": np.ascontiguousarray(np.asarray(L0_attn_Wv, np.float32)),
        "wo": np.ascontiguousarray(np.asarray(L0_attn_Wo, np.float32)),
        "bo": np.ascontiguousarray(np.asarray(L0_attn_bo, np.float32).reshape(1, D)),
        "wr": np.ascontiguousarray(np.asarray(L0_ffn_W_r, np.float32)),
        "wk2": np.ascontiguousarray(np.asarray(L0_ffn_W_k, np.float32)),
        "wv2": np.ascontiguousarray(np.asarray(L0_ffn_W_v, np.float32)),
        "mur": np.ascontiguousarray(np.asarray(L0_ffn_mu_r, np.float32).reshape(8, 128).T),
        "muk": np.ascontiguousarray(np.asarray(L0_ffn_mu_k, np.float32).reshape(8, 128).T),
    }
    if _NC_CACHE is None:
        _NC_CACHE = build_nc()
    nc = _NC_CACHE
    in_maps = [dict(w, x=np.ascontiguousarray(x[i])) for i in range(B)]
    res = run_bass_kernel_spmd(nc, in_maps, core_ids=list(range(B)))
    out = np.stack([np.asarray(res.results[i]["y"]) for i in range(B)], axis=0)
    return out.astype(np.float32)


# version 31 (final)
# speedup vs baseline: 1.0821x; 1.0821x over previous
import numpy as np

import concourse.bass as bass
import concourse.mybir as mybir
from concourse.bass_utils import run_bass_kernel_spmd

F32 = mybir.dt.float32
BF16 = mybir.dt.bfloat16
Act = mybir.ActivationFunctionType
Alu = mybir.AluOpType
AX = mybir.AxisListType

T, D, H, DH = 4096, 1024, 16, 64
CH = 512
NCH = T // CH
NTT = CH // 128
NJC = T // 128
NG = NJC // 2
SEG = 516
EPS = 1e-5
NHERON = 8


class Sem:
    def __init__(self, cm):
        self._cm = cm
        self.h = cm.__enter__()
        self.n = 0

    def inc(self, instr, amount=1):
        instr.then_inc(self.h, amount)
        self.n += amount
        return self.n


def build_nc():
    nc = bass.Bass("TRN2", target_bir_lowering=False, debug=False)

    x_d = nc.dram_tensor("x", [T, D], F32, kind="ExternalInput").ap()
    wq_d = nc.dram_tensor("wq", [D, D], F32, kind="ExternalInput").ap()
    wk_d = nc.dram_tensor("wk", [D, D], F32, kind="ExternalInput").ap()
    wv_d = nc.dram_tensor("wv", [D, D], F32, kind="ExternalInput").ap()
    wo_d = nc.dram_tensor("wo", [D, D], F32, kind="ExternalInput").ap()
    bo_d = nc.dram_tensor("bo", [1, D], F32, kind="ExternalInput").ap()
    wr_d = nc.dram_tensor("wr", [D, D], F32, kind="ExternalInput").ap()
    wk2_d = nc.dram_tensor("wk2", [D, D], F32, kind="ExternalInput").ap()
    wv2_d = nc.dram_tensor("wv2", [D, D], F32, kind="ExternalInput").ap()
    mur_d = nc.dram_tensor("mur", [128, 8], F32, kind="ExternalInput").ap()
    muk_d = nc.dram_tensor("muk", [128, 8], F32, kind="ExternalInput").ap()
    y_d = nc.dram_tensor("y", [T, D], F32, kind="ExternalOutput").ap()
    ks_d = nc.dram_tensor("ks", [D, T], BF16, kind="Internal").ap()
    vs_d = nc.dram_tensor("vs", [H, 128, NJC * DH], BF16, kind="Internal").ap()
    xns_d = nc.dram_tensor("xns", [128, 8 * T], BF16, kind="Internal").ap()

    A = nc.alloc_sbuf_tensor
    wq_sb = A("wq_sb", [128, 8192], BF16).ap()
    wo_sb = A("wo_sb", [128, 8192], BF16).ap()
    wr_t = A("wr_t", [128, 8192], BF16).ap()
    wk2_t = A("wk2_t", [128, 8192], BF16).ap()
    wv2_t = A("wv2_t", [128, 8192], BF16).ap()
    regA = A("regA", [128, 32768], mybir.dt.uint8).ap()
    regA_bf = regA.bitcast(BF16)
    regA_f32 = regA.bitcast(F32)
    wkp_sb = regA_bf[:, 0:8192]
    wvp_sb = regA_bf[:, 8192:16384]
    xr_sb = regA_bf[:, 0:4096]
    xk_sb = regA_bf[:, 4096:8192]
    kact_sb = regA_bf[:, 8192:12288]
    tanh_f = regA_f32[:, 6144:7168]
    tmp_f = regA_f32[:, 7168:8192]

    x_sb = A("x_sb", [128, 4096], F32).ap()
    qz = x_sb.bitcast(BF16)
    xb = [qz[:, 0:4096], qz[:, 4096:8192]]
    xn_fm = A("xn_fm", [128, 8 * SEG], BF16).ap()
    q_sb = A("q_sb", [128, 4096], BF16).ap()
    kh0 = A("kh0", [128, 4096], BF16).ap()
    kh1 = A("kh1", [128, 4096], BF16).ap()
    vhe = A("vhe", [128, 2080], BF16).ap()
    vho = A("vho", [128, 4096], BF16).ap()
    p_sb = A("p_sb", [128, 4096], BF16).ap()
    ofm = A("ofm", [128, 4096], BF16).ap()
    wtmp = ofm[:, 0:1024]
    of32 = A("of32", [128, 512], F32).ap()
    rl_sb = A("rl_sb", [128, 512], BF16).ap()
    xn_tm = of32.bitcast(BF16)[:, 0:1024]
    q2_sb = rl_sb
    out_sb = A("out_sb", [128, 2048], F32).ap()
    wld0 = out_sb[:, 0:1024]
    wld1 = out_sb[:, 1024:2048]
    xres = A("xres", [128, 1024], F32).ap()
    k_sb = kh0[:, 0:1024]
    v_sb = kh0[:, 1024:2048]
    k2_sb = kh0[:, 2048:3584]
    ident = A("ident", [128, 128], BF16).ap()
    ones_sb = A("ones_sb", [128, 128], BF16).ap()
    selp_sb = A("selp_sb", [128, 2], BF16).ap()
    mur_sb = A("mur_sb", [128, 8], F32).ap()
    muk_sb = A("muk_sb", [128, 8], F32).ap()
    bo_bf = A("bo_bf", [1, 1024], BF16).ap()
    negb = A("negb", [128, 16], F32).ap()
    mq2r = A("mq2r", [1, 16], F32).ap()
    mk2r = A("mk2r", [1, 16], F32).ap()
    mk2g = A("mk2g", [1, 128], F32).ap()
    nbr = A("nbr", [1, 16], BF16).ap()
    tr1 = A("tr1", [1, 16], F32).ap()
    stat = A("stat", [128, 32], F32).ap()
    m_c, ex_c = stat[:, 0:4], stat[:, 4:8]
    vv_c, vh_c = stat[:, 8:12], stat[:, 12:16]
    s_c, r_c = stat[:, 16:20], stat[:, 20:24]
    t1_c, rstd_c = stat[:, 24:28], stat[:, 28:32]

    pA = nc.alloc_psum_tensor("pA", [128, 1024], F32).ap()
    pB = nc.alloc_psum_tensor("pB", [128, 1024], F32).ap()
    pO = nc.alloc_psum_tensor("pO", [128, 1024], F32).ap()
    pC = nc.alloc_psum_tensor("pC", [128, 1024], F32).ap()
    pE = pC[:, 0:512]
    pTs = [pC.bitcast(BF16)[:, 0:1024],
           pC.bitcast(BF16)[:, 1024:2048]]
    pT = pTs[1]

    sems = []

    def mk(name):
        s = Sem(nc.semaphore(name))
        sems.append(s)
        return s

    sW = mk("sW")
    sX = mk("sX")
    sXR = mk("sXR")
    sXN = mk("sXN")
    sKH = mk("sKH")
    sVH = mk("sVH")
    sSP = mk("sSP")
    sOUT = mk("sOUT")
    sT = mk("sT")
    sS = mk("sS")
    sPV = mk("sPV")
    sE = mk("sE")
    sV = mk("sV")
    sSC = mk("sSC")
    sRL = mk("sRL")
    sRLB = mk("sRLB")
    sOEV = mk("sOEV")
    sG = mk("sG")

    te, ve, se, sy, gp = nc.tensor, nc.vector, nc.scalar, nc.sync, nc.gpsimd

    xn3 = xn_fm.rearrange("p (s x) -> p s x", x=SEG)
    pT3 = pT.rearrange("p (s x) -> p s x", x=128)

    with nc.Block() as block:
        sW.inc(sy.dma_start(out=mur_sb[:, :], in_=mur_d[:, :]), 16)
        sW.inc(sy.dma_start(out=muk_sb[:, :], in_=muk_d[:, :]), 16)
        sW.inc(sy.dma_start(out=wld1[0:1, :], in_=bo_d[0:1, :]), 16)
        ve.wait_ge(sW.h, sW.n)
        ve.tensor_copy(bo_bf[0:1, :], wld1[0:1, :])
        ve.memset(mk2r[0:1, :], 0.0)
        sV.inc(ve.memset(xn3[:, :, 0:1], 0.0))

        gp.memset(ident[:, :], 0.0)
        gp.drain()
        gp.affine_select(out=ident[:, :], in_=ident[:, :], compare_op=Alu.not_equal,
                         fill=1.0, base=0, pattern=[[-1, 128]], channel_multiplier=1)
        gp.memset(ones_sb[:, :], 1.0)
        gp.drain()
        gp.memset(selp_sb[:, :], 0.0)
        gp.drain()
        gp.memset(selp_sb[0:64, 0:1], 1.0)
        ident_ready = sG.inc(gp.memset(selp_sb[64:128, 1:2], 1.0))
        gp.memset(vhe[:, :], 0.0)
        gp.memset(vho[:, :], 0.0)
        vhe3 = vhe.rearrange("p (jc c) -> p jc c", c=65)
        vho3 = vho.rearrange("p (jc c) -> p jc c", c=128)
        gp.memset(vhe3[:, :, 64:65], 1.0)
        sG.inc(gp.memset(vho3[:, :, 32:33], 1.0))
        vh_ready = sG.n

        for wd, wsb in [(wq_d, wq_sb), (wo_d, wo_sb), (wk_d, wkp_sb), (wv_d, wvp_sb)]:
            sW.inc(gp.dma_start(
                out=wsb.rearrange("p (c x) -> p c x", x=1024),
                in_=wd.rearrange("(c p) x -> p c x", p=128)), 16)
        ve.wait_ge(sW.h, sW.n)
        te.wait_ge(sG.h, ident_ready)
        for wd, wsb, scale in [(wr_d, wr_t, None), (wk2_d, wk2_t, None), (wv2_d, wv2_t, None)]:
            wsb3 = wsb.rearrange("p (s x) -> p s x", x=1024)
            for dc in range(8):
                wslot = wtmp if dc % 2 == 0 else ofm[:, 1024:2048]
                if scale is None:
                    gp.wait_ge(sT.h, sT.n)
                    sW.inc(gp.dma_start(out=wslot,
                                        in_=wd[dc * 128:(dc + 1) * 128, :]), 16)
                    te.wait_ge(sW.h, sW.n)
                else:
                    slot = wld0 if dc % 2 == 0 else wld1
                    sy.wait_ge(sV.h, sV.n)
                    sW.inc(sy.dma_start(out=slot[:, :],
                                        in_=wd[dc * 128:(dc + 1) * 128, :]), 16)
                    ve.wait_ge(sW.h, sW.n)
                    sV.inc(ve.tensor_scalar_mul(wslot, slot[:, :], scale))
                    te.wait_ge(sV.h, sV.n)
                pts = pTs[dc % 2]
                if dc >= 2:
                    te.wait_ge(sV.h, sV.n)
                for ct in range(8):
                    ti = te.transpose(pts[:, ct * 128:(ct + 1) * 128],
                                      wslot[:, ct * 128:(ct + 1) * 128], ident[:, :])
                sT.inc(ti)
                ve.wait_ge(sT.h, sT.n)
                sV.inc(ve.tensor_copy(wsb3[:, :, dc * 128:dc * 128 + 128],
                                      pts.rearrange("p (s x) -> p s x", x=128)))

        spill_mark = [0]
        transp_done = {}
        evac_done = {}

        def emit_ln_xn(x_wait_n, sink, slot):
            ve.wait_ge(sX.h, x_wait_n)
            for tt in range(NTT):
                xt = xb[slot][:, tt * 1024:(tt + 1) * 1024]
                ve.reduce_sum(m_c[:, tt:tt + 1], xt, axis=AX.X)
                ve.tensor_tensor(sink, xt, xt, op=Alu.mult)
                ve.drain()
                ve.reduce_sum(ex_c[:, tt:tt + 1], sink, axis=AX.X)
            ve.drain()
            ve.tensor_scalar_mul(ex_c[:, :], ex_c[:, :], 1.0 / 1024.0)
            ve.tensor_scalar_mul(m_c[:, :], m_c[:, :], 1.0 / 1024.0)
            ve.drain()
            ve.tensor_tensor(t1_c[:, :], m_c[:, :], m_c[:, :], op=Alu.mult)
            ve.drain()
            ve.tensor_tensor(vv_c[:, :], ex_c[:, :], t1_c[:, :], op=Alu.subtract)
            ve.drain()
            ve.tensor_scalar_add(vv_c[:, :], vv_c[:, :], EPS)
            ve.drain()
            ve.tensor_scalar_mul(vh_c[:, :], vv_c[:, :], 0.5)
            ve.tensor_scalar(s_c[:, :], vv_c[:, :], 0.5, 0.5, op0=Alu.mult, op1=Alu.add)
            ve.drain()
            for _ in range(NHERON):
                ve.reciprocal(r_c[:, :], s_c[:, :])
                ve.drain()
                ve.tensor_tensor(t1_c[:, :], vh_c[:, :], r_c[:, :], op=Alu.mult)
                ve.drain()
                ve.scalar_tensor_tensor(s_c[:, :], s_c[:, :], 0.5, t1_c[:, :],
                                        op0=Alu.mult, op1=Alu.add)
                ve.drain()
            ve.reciprocal(rstd_c[:, :], s_c[:, :])
            ve.drain()
            last_read = None
            for tt in range(NTT):
                xt = xb[slot][:, tt * 1024:(tt + 1) * 1024]
                xnt = q_sb[:, (tt % 2) * 1024:(tt % 2) * 1024 + 1024]
                if tt > 1:
                    ve.wait_ge(sT.h, transp_done[tt - 2])
                i = ve.tensor_scalar(xnt, xt, m_c[:, tt:tt + 1],
                                     rstd_c[:, tt:tt + 1],
                                     op0=Alu.subtract, op1=Alu.mult)
                last_read = sV.inc(i)
                te.wait_ge(sV.h, sV.n)
                if tt > 1:
                    te.wait_ge(sSC.h, evac_done[tt - 2])
                pts = pTs[tt % 2]
                for c in range(8):
                    ti = te.transpose(pts[:, c * 128:(c + 1) * 128],
                                      xnt[:, c * 128:(c + 1) * 128], ident[:, :])
                transp_done[tt] = sT.inc(ti)
                se.wait_ge(sT.h, transp_done[tt])
                if tt == 0:
                    se.wait_ge(sSP.h, spill_mark[0])
                evac_done[tt] = sSC.inc(
                    se.activation(xn3[:, :, 1 + tt * 128: 1 + tt * 128 + 128],
                                  pts.rearrange("p (s x) -> p s x", x=128),
                                  Act.Copy))
            return last_read

        x_wait = []
        xread_done = {}
        gp.wait_ge(sG.h, sG.n)
        sX.inc(gp.dma_start(out=xb[0].rearrange("p (tt c) -> p tt c", c=1024),
                            in_=x_d[0:512, :].rearrange("(tt p) c -> p tt c", p=128)), 16)
        x_wait.append(sX.n)
        sX.inc(gp.dma_start(out=xb[1].rearrange("p (tt c) -> p tt c", c=1024),
                            in_=x_d[512:1024, :].rearrange("(tt p) c -> p tt c", p=128)), 16)
        x_wait.append(sX.n)

        for ch in range(NCH):
            last_x_read = emit_ln_xn(x_wait[ch], out_sb[:, 0:1024], ch % 2)
            xread_done[ch] = last_x_read
            if ch + 2 < NCH:
                gp.wait_ge(sV.h, xread_done[ch])
                base = (ch + 2) * CH
                sX.inc(gp.dma_start(
                    out=xb[ch % 2].rearrange("p (tt c) -> p tt c", c=1024),
                    in_=x_d[base:base + 512, :].rearrange("(tt p) c -> p tt c", p=128)), 16)
                x_wait.append(sX.n)

            gp.wait_ge(sSC.h, sSC.n)
            xns3 = xns_d.rearrange("p (s t) -> p s t", t=T)
            sSP.inc(gp.dma_start(out=xns3[:, :, ch * 512:(ch + 1) * 512],
                                 in_=xn3[:, :, 1:513]), 16)
            spill_mark[0] = sSP.n

            kev = {}
            kspill = {}
            bnd_done = {}
            rmx_done = {}

            def emit_b_bounds(dp):
                for par in range(2):
                    h = dp * 2 + par
                    te.wait_ge(sSC.h, kev[dp])
                    if h >= 2:
                        te.wait_ge(sV.h, rmx_done[h - 2])
                    pes = pO[:, (h % 2) * 512:(h % 2) * 512 + 512]
                    bm = te.matmul(pes[0:1, :], selp_sb[:, par:par + 1],
                                   k2_sb[:, (dp % 3) * 512:(dp % 3) * 512 + 512],
                                   start=True, stop=True)
                    bm_done = sT.inc(bm)
                    ve.wait_ge(sT.h, bm_done)
                    rmx_done[h] = sV.inc(
                        ve.reduce_max(mk2g[0:1, ch * 16 + h: ch * 16 + h + 1],
                                      pes[0:1, :], axis=AX.X))
                bnd_done[dp] = sT.n

            for dp in range(8):
                half = pA[:, 0:512] if dp % 2 == 0 else pA[:, 512:1024]
                if dp == 0:
                    te.wait_ge(sSC.h, sSC.n)
                if dp >= 2:
                    te.wait_ge(sSC.h, kev[dp - 2])
                for c in range(8):
                    mi = te.matmul(half,
                                   wkp_sb[:, c * 1024 + dp * 128: c * 1024 + dp * 128 + 128],
                                   xn3[:, c, 1:513],
                                   start=(c == 0), stop=(c == 7))
                kp_done = sT.inc(mi)
                if dp >= 2:
                    emit_b_bounds(dp - 2)
                kslot = k_sb[:, (dp % 2) * 512:(dp % 2) * 512 + 512]
                se.wait_ge(sT.h, kp_done)
                if dp >= 2:
                    se.wait_ge(sSP.h, kspill[dp - 2])
                if dp >= 3:
                    se.wait_ge(sT.h, bnd_done[dp - 3])
                sSC.inc(se.activation(kslot, half, Act.Copy))
                kev[dp] = sSC.inc(se.activation(
                    k2_sb[:, (dp % 3) * 512:(dp % 3) * 512 + 512], kslot, Act.Square))
                gp.wait_ge(sSC.h, kev[dp])
                sSP.inc(gp.dma_start(
                    out=ks_d[dp * 128:(dp + 1) * 128, ch * 512:(ch + 1) * 512],
                    in_=kslot), 16)
                kspill[dp] = sSP.n
            emit_b_bounds(6)
            emit_b_bounds(7)

            for tt in range(NTT):
                g = ch * NTT + tt
                te.wait_ge(sSC.h, sSC.n)
                for c in range(8):
                    for nh in range(2):
                        mi = te.matmul(pB[:, nh * 512:(nh + 1) * 512],
                                       xn3[:, c, 1 + tt * 128: 1 + tt * 128 + 128],
                                       wvp_sb[:, c * 1024 + nh * 512: c * 1024 + (nh + 1) * 512],
                                       start=(c == 0), stop=(c == 7))
                vp_done = sT.inc(mi)
                se.wait_ge(sT.h, vp_done)
                se.wait_ge(sSP.h, sSP.n)
                vev = sSC.inc(se.activation(v_sb[:, :], pB[:, :], Act.Copy))
                gp.wait_ge(sSC.h, vev)
                sSP.inc(gp.dma_start(
                    out=vs_d[:, :, g * 64:(g + 1) * 64].transpose([1, 0, 2]),
                    in_=v_sb.rearrange("j (h c) -> j h c", c=64)), 16)

        mk2g3 = mk2g.rearrange("p (c h) -> p c h", h=16)
        ve.drain()
        for h2 in range(H):
            ve.reduce_max(mk2r[0:1, h2:h2 + 1], mk2g3[0:1, :, h2], axis=AX.X)
        ve.drain()
        spill_total = sSP.n

        xns3 = xns_d.rearrange("p (s t) -> p s t", t=T)
        sy.wait_ge(sSP.h, sSP.n)
        sy.wait_ge(sV.h, sV.n)
        sXN.inc(sy.dma_start(out=xn3[:, :, 1:513], in_=xns3[:, :, 0:512]), 16)
        xn_wait = [sXN.n]
        ve.memset(qz[:, :], 0.0)
        ve.drain()
        sXR.inc(sy.dma_start(out=xres[:, :], in_=x_d[0:128, :]), 16)
        xres_wait = sXR.n

        kh_bufs = [kh0, kh1]
        kh_wait = {}
        s_pair_end = {}

        def load_kh(pg):
            p = pg % 8
            if pg >= 2:
                sy.wait_ge(sS.h, s_pair_end[pg - 2])
            sKH.inc(sy.dma_start(out=kh_bufs[pg % 2][:, :],
                                 in_=ks_d[p * 128:(p + 1) * 128, :]), 16)
            kh_wait[pg] = sKH.n

        vh_wait = {}
        pv_end = {}

        def load_vh(hg):
            h = hg % H
            if h % 2 == 0:
                dst = vhe.rearrange("p (jc c) -> p jc c", c=65)[:, :, 0:64]
            else:
                dst = vho.rearrange("p (jc c) -> p jc c", c=128)[:, :, 64:128]
            src = vs_d[h].rearrange("j (jc c) -> j jc c", c=64)
            if hg >= 2:
                gp.wait_ge(sPV.h, pv_end[hg - 2])
            sVH.inc(gp.dma_start(out=dst[:, 0:16, :], in_=src[:, 0:16, :]), 16)
            sVH.inc(gp.dma_start(out=dst[:, 16:32, :], in_=src[:, 16:32, :]), 16)
            vh_wait[hg] = sVH.n

        sy.wait_ge(sSP.h, spill_total)
        gp.wait_ge(sSP.h, spill_total)
        gp.wait_ge(sG.h, vh_ready)
        ve.wait_ge(sT.h, sT.n)
        load_kh(0)
        load_vh(0)
        load_vh(1)

        evac_end = {}
        recip_rdy = {}
        rank1_rdy = {}
        negb_cell = [0]

        def emit_rank1(hg):
            te.wait_ge(sRL.h, recip_rdy[hg])
            lr = 64 if hg % 2 == 0 else 32
            dst = pO[:, (hg % 2) * 512:(hg % 2) * 512 + 512]
            r1 = te.matmul(dst, ones_sb[lr:lr + 1, 0:128],
                           rl_sb[lr:lr + 1, 0:512], start=True, stop=True)
            rank1_rdy[hg] = sRLB.inc(r1)

        def emit_tt_norm(hg):
            h = hg % H
            p = h // 2
            orows = slice(0, 64) if h % 2 == 0 else slice(64, 128)
            src = pO[:, (hg % 2) * 512:(hg % 2) * 512 + 512]
            ve.wait_ge(sRLB.h, rank1_rdy[hg])
            i = ve.tensor_tensor(ofm[orows, p * 512:(p + 1) * 512],
                                 of32[orows, :], src[orows, :], op=Alu.mult)
            evac_end[hg] = sOEV.inc(i)

        ffn_prep_prev = 0
        tensor_ffn_prev = 0
        for ch in range(NCH):
            if ch > 0:
                te.wait_ge(sOEV.h, sOEV.n)
                te.wait_ge(sSC.h, sSC.n)
            te.wait_ge(sXN.h, xn_wait[ch])
            q_evac = {}
            sq_done = {}
            qp_done = {}
            bounds_done = {}
            redmax_done = {}

            def emit_qmm(dp):
                half = pA[:, 0:512] if dp % 2 == 0 else pA[:, 512:1024]
                if dp >= 2:
                    te.wait_ge(sV.h, q_evac[dp - 2])
                if dp >= 4:
                    se.wait_ge(sT.h, bounds_done[dp - 4])
                for c in range(8):
                    mi = te.matmul(half,
                                   wq_sb[:, c * 1024 + dp * 128: c * 1024 + dp * 128 + 128],
                                   xn3[:, c, 1:513],
                                   start=(c == 0), stop=(c == 7))
                qp_done[dp] = sT.inc(mi)
                ve.wait_ge(sT.h, qp_done[dp])
                h0, h1 = 2 * dp, 2 * dp + 1
                ve.tensor_copy(qz[0:64, h0 * 512:(h0 + 1) * 512], half[0:64, :])
                q_evac[dp] = sV.inc(
                    ve.tensor_copy(qz[64:128, h1 * 512:(h1 + 1) * 512], half[64:128, :]))
                se.wait_ge(sV.h, q_evac[dp])
                i0 = se.activation(p_sb[:, (h0 % 8) * 512:(h0 % 8) * 512 + 512],
                                   qz[:, h0 * 512:(h0 + 1) * 512], Act.Square)
                sSC.inc(i0)
                sq_done[dp] = sSC.inc(
                    se.activation(p_sb[:, (h1 % 8) * 512:(h1 % 8) * 512 + 512],
                                  qz[:, h1 * 512:(h1 + 1) * 512], Act.Square))

            def emit_bounds(dp):
                for par in range(2):
                    h = dp * 2 + par
                    rows = slice(0, 64) if par == 0 else slice(64, 128)
                    te.wait_ge(sSC.h, sq_done[dp])
                    if dp >= 1 or par == 1:
                        te.wait_ge(sV.h, redmax_done[h - 2] if h >= 2 else 0)
                    q2slot = p_sb[:, (h % 8) * 512:(h % 8) * 512 + 512]
                    pes = pC[:, (h % 2) * 512:(h % 2) * 512 + 512]
                    bm = te.matmul(pes[0:1, :], ones_sb[rows, 0:1],
                                   q2slot[rows, :], start=True, stop=True)
                    bm_done = sT.inc(bm)
                    ve.wait_ge(sT.h, bm_done)
                    redmax_done[h] = sV.inc(
                        ve.reduce_max(mq2r[0:1, h:h + 1], pes[0:1, :], axis=AX.X))
                    ve.drain()
                bounds_done[dp] = sT.n

            for dp in range(4):
                emit_qmm(dp)
            for dp in range(4, 8):
                emit_bounds(dp - 4)
                emit_qmm(dp)
            for dp in range(4, 8):
                emit_bounds(dp)
            if True:
                    pass
            ve.drain()
            ve.tensor_tensor(tr1[0:1, :], mq2r[0:1, :], mk2r[0:1, :], op=Alu.add)
            ve.drain()
            nb_done = sV.inc(ve.tensor_scalar_mul(nbr[0:1, :], tr1[0:1, :], -1.0 / 16.0))
            ve.drain()
            te.wait_ge(sV.h, nb_done)
            nb_mm_done = sT.inc(te.matmul(pE[:, 0:16], ones_sb[0:1, 0:128],
                                          nbr[0:1, 0:16], start=True, stop=True))
            ve.wait_ge(sT.h, nb_mm_done)
            negb_done = sV.inc(ve.tensor_copy(negb[:, :], pE[:, 0:16]))
            negb_cell[0] = negb_done

            ve.wait_ge(sT.h, tensor_ffn_prev)
            last = None
            for sg in range(8):
                dseg = kact_sb[:, sg * 512:(sg + 1) * 512]
                ve.tensor_tensor(dseg, xn3[:, sg, 1:513], xn3[:, sg, 0:512],
                                 op=Alu.subtract)
                ve.drain()
                ve.scalar_tensor_tensor(xr_sb[:, sg * 512:(sg + 1) * 512],
                                        dseg, mur_sb[:, sg:sg + 1],
                                        xn3[:, sg, 0:512],
                                        op0=Alu.mult, op1=Alu.add)
                last = ve.scalar_tensor_tensor(xk_sb[:, sg * 512:(sg + 1) * 512],
                                               dseg, muk_sb[:, sg:sg + 1],
                                               xn3[:, sg, 0:512],
                                               op0=Alu.mult, op1=Alu.add)
            ffn_prep = sV.inc(last)

            sc_mark = sSC.n
            NGC = H * NG
            sregs = [pA, pB, pC]
            s_done = {}
            exp_done = {}
            pv_done = {}

            def head_entry_waits(h):
                hg = ch * H + h
                pg = ch * 8 + h // 2
                if h % 2 == 0 and pg + 1 < NCH * 8:
                    load_kh(pg + 1)
                te.wait_ge(sV.h, q_evac[7] if h == 0 else q_evac[h // 2])
                te.wait_ge(sKH.h, kh_wait[pg])
                te.wait_ge(sVH.h, vh_wait[hg])

            def emit_S(g):
                h = g // NG
                lg = g % NG
                if lg == 0:
                    head_entry_waits(h)
                khb = kh_bufs[(ch * 8 + h // 2) % 2]
                reg = sregs[g % 3]
                for half in range(2):
                    jc = 2 * lg + half
                    mi = te.matmul(reg[:, half * 512:(half + 1) * 512],
                                   khb[:, jc * 128:(jc + 1) * 128],
                                   qz[:, h * 512:(h + 1) * 512],
                                   start=True, stop=True)
                s_done[g] = sS.inc(mi)
                if lg == NG - 1 and h % 2 == 1:
                    s_pair_end[ch * 8 + h // 2] = sS.n

            def emit_exp(g):
                h = g // NG
                reg = sregs[g % 3]
                se.wait_ge(sS.h, s_done[g])
                if g == 0:
                    se.wait_ge(sV.h, negb_done)
                if g >= 4:
                    se.wait_ge(sPV.h, pv_done[g - 4])
                slot = p_sb[:, (g % 4) * 1024:(g % 4 + 1) * 1024]
                ei = se.activation(slot, reg[:, :], Act.Exp,
                                   bias=negb[:, h:h + 1], scale=0.125)
                exp_done[g] = sE.inc(ei)

            def emit_PV(g):
                h = g // NG
                hg = ch * H + h
                lg = g % NG
                if lg == 0 and hg >= 2:
                    te.wait_ge(sOEV.h, evac_end[hg - 2])
                vhb = vhe if h % 2 == 0 else vho
                ohalf = pO[:, 0:512] if h % 2 == 0 else pO[:, 512:1024]
                slot = p_sb[:, (g % 4) * 1024:(g % 4 + 1) * 1024]
                for half in range(2):
                    jc = 2 * lg + half
                    if h % 2 == 0:
                        mi = te.matmul(ohalf[0:65, :],
                                       vhb[:, jc * 65:(jc + 1) * 65],
                                       slot[:, half * 512:(half + 1) * 512],
                                       start=(jc == 0), stop=(jc == NJC - 1))
                    else:
                        mi = te.matmul(ohalf,
                                       vhb[:, jc * 128:(jc + 1) * 128],
                                       slot[:, half * 512:(half + 1) * 512],
                                       start=(jc == 0), stop=(jc == NJC - 1))
                pv_done[g] = sPV.inc(mi)

            def emit_head_epilogue(h):
                hg = ch * H + h
                pv_end[hg] = pv_done[(h + 1) * NG - 1]
                if hg + 2 < NCH * H:
                    load_vh(hg + 2)
                if hg >= 1 and (hg - 1) not in evac_end:
                    emit_tt_norm(hg - 1)
                rows = slice(0, 64) if h % 2 == 0 else slice(64, 128)
                ohalf = pO[:, 0:512] if h % 2 == 0 else pO[:, 512:1024]
                lrow = 64 if h % 2 == 0 else 32
                ve.wait_ge(sPV.h, pv_end[hg])
                if h == 0:
                    ve.wait_ge(sSC.h, sc_mark)
                ve.tensor_copy(of32[rows, :], ohalf[rows, :])
                with nc.allow_low_precision(reason="1/l feeds a bf16 matmul"):
                    ve.reciprocal(rl_sb[lrow:lrow + 1, :], ohalf[lrow:lrow + 1, :])
                ve.drain()
                recip_rdy[hg] = sRL.inc(ve.engine_nop())

            for gg in range(0, NGC, 2):
                if gg >= 2:
                    te.wait_ge(sE.h, exp_done[gg - 2])
                emit_S(gg)
                emit_S(gg + 1)
                for g2 in (gg - 3, gg - 2):
                    if g2 >= 0:
                        emit_PV(g2)
                        if g2 % NG == NG - 1:
                            emit_head_epilogue(g2 // NG)
                emit_exp(gg)
                emit_exp(gg + 1)
                for sub in (gg, gg + 1):
                    if sub % NG == 9:
                        hprev = ch * H + sub // NG - 1
                        if hprev >= 0 and hprev not in rank1_rdy:
                            emit_rank1(hprev)
            te.wait_ge(sE.h, exp_done[NGC - 1])
            for g2 in (NGC - 3, NGC - 2, NGC - 1):
                emit_PV(g2)
                if g2 % NG == NG - 1:
                    emit_head_epilogue(g2 // NG)

            emit_rank1(ch * H + H - 1)
            emit_tt_norm(ch * H + H - 1)
            all_evac = evac_end[ch * H + H - 1]

            te.wait_ge(sOEV.h, all_evac)
            te.wait_ge(sV.h, ffn_prep)
            kact_evac = {}
            for cp in range(8):
                half = pA[:, 0:512] if cp % 2 == 0 else pA[:, 512:1024]
                if cp >= 2:
                    te.wait_ge(sSC.h, kact_evac[cp - 2])
                for c in range(8):
                    mi = te.matmul(half,
                                   wk2_t[:, c * 1024 + cp * 128: c * 1024 + cp * 128 + 128],
                                   xk_sb[:, c * 512:(c + 1) * 512],
                                   start=(c == 0), stop=(c == 7))
                km_done = sT.inc(mi)
                ve.wait_ge(sT.h, km_done)
                if cp >= 1:
                    ve.wait_ge(sSC.h, kact_evac[cp - 1])
                relu_done = sV.inc(ve.tensor_scalar_max(of32[:, :], half, 0.0))
                se.wait_ge(sV.h, relu_done)
                kact_evac[cp] = sSC.inc(
                    se.activation(kact_sb[:, cp * 512:(cp + 1) * 512],
                                  of32[:, :], Act.Square))

            tanh_done = {}
            comb_stt = {}
            comb_u1 = {}
            for tt in range(NTT):
                if tt == 0:
                    te.wait_ge(sSC.h, kact_evac[7])
                else:
                    te.wait_ge(sV.h, comb_u1[tt - 1])
                for half in range(2):
                    dst = pA[:, half * 512:(half + 1) * 512]
                    for dpc in range(8):
                        te.matmul(dst,
                                  ofm[:, dpc * 512 + tt * 128: dpc * 512 + tt * 128 + 128],
                                  wo_sb[:, dpc * 1024 + half * 512: dpc * 1024 + (half + 1) * 512],
                                  start=(dpc == 0), stop=False)
                    mi = te.matmul(dst, ones_sb[0:1, 0:128],
                                   bo_bf[0:1, half * 512:(half + 1) * 512],
                                   start=False, stop=True)
                oproj_done = sT.inc(mi)
                glob_tt = ch * NTT + tt
                oslot = out_sb[:, (glob_tt % 2) * 1024:(glob_tt % 2 + 1) * 1024]
                ve.wait_ge(sT.h, oproj_done)
                if glob_tt >= 2:
                    ve.wait_ge(sOUT.h, 16 * (glob_tt - 1))
                ve.wait_ge(sXR.h, xres_wait)
                comb_u1[tt] = sV.inc(ve.tensor_tensor(oslot, xres[:, :], pA[:, :],
                                                      op=Alu.add))
                ve.drain()
                ntt = glob_tt + 1
                if ntt < NCH * NTT:
                    sy.wait_ge(sV.h, comb_u1[tt])
                    sXR.inc(sy.dma_start(out=xres[:, :],
                                         in_=x_d[ntt * 128:(ntt + 1) * 128, :]), 16)
                    xres_wait = sXR.n

                if tt > 0:
                    te.wait_ge(sSC.h, tanh_done[tt - 1])
                for c in range(8):
                    for half in range(2):
                        mi = te.matmul(pB[:, half * 512:(half + 1) * 512],
                                       xr_sb[:, c * 512 + tt * 128: c * 512 + tt * 128 + 128],
                                       wr_t[:, c * 1024 + half * 512: c * 1024 + (half + 1) * 512],
                                       start=(c == 0), stop=(c == 7))
                rproj_done = sT.inc(mi)

                if tt > 0:
                    te.wait_ge(sV.h, comb_stt[tt - 1])
                for cp in range(8):
                    for half in range(2):
                        mi = te.matmul(pO[:, half * 512:(half + 1) * 512],
                                       kact_sb[:, cp * 512 + tt * 128: cp * 512 + tt * 128 + 128],
                                       wv2_t[:, cp * 1024 + half * 512: cp * 1024 + (half + 1) * 512],
                                       start=(cp == 0), stop=(cp == 7))
                vproj_done = sT.inc(mi)

                se.wait_ge(sT.h, rproj_done)
                if tt > 0:
                    se.wait_ge(sV.h, comb_stt[tt - 1])
                tanh_done[tt] = sSC.inc(se.activation(tanh_f, pB[:, :],
                                                      Act.Tanh, scale=0.5))

                ve.wait_ge(sT.h, vproj_done)
                ve.wait_ge(sSC.h, tanh_done[tt])
                comb_stt[tt] = sV.inc(
                    ve.scalar_tensor_tensor(tmp_f, tanh_f, 1.0,
                                            pO[:, :], op0=Alu.add, op1=Alu.mult))
                ve.drain()
                comb_fin = sV.inc(ve.scalar_tensor_tensor(oslot, tmp_f, 0.5, oslot,
                                                          op0=Alu.mult, op1=Alu.add))
                ve.drain()

                gp.wait_ge(sV.h, comb_fin)
                base = ch * CH + tt * 128
                sOUT.inc(gp.dma_start(out=y_d[base:base + 128, :], in_=oslot), 16)
                if tt == NTT - 1:
                    tensor_ffn_prev = sT.n
                if tt == 1 and ch + 1 < NCH:
                    sy.wait_ge(sV.h, ffn_prep)
                    base = (ch + 1) * 512
                    sXN.inc(sy.dma_start(out=xn3[:, :, 0:513],
                                         in_=xns3[:, :, base - 1: base + 512]), 16)
                    xn_wait.append(sXN.n)

        gp.wait_ge(sOUT.h, sOUT.n)

    for s in sems:
        s._cm.__exit__(None, None, None)
    return nc


_NC_CACHE = None


def kernel(L0_attn_x, L0_attn_Wq, L0_attn_Wk, L0_attn_Wv, L0_attn_Wo, L0_attn_bo,
           L0_ffn_W_r, L0_ffn_W_k, L0_ffn_W_v, L0_ffn_mu_r, L0_ffn_mu_k):
    global _NC_CACHE
    x = np.ascontiguousarray(np.asarray(L0_attn_x, dtype=np.float32))
    B = x.shape[0]
    w = {
        "wq": np.ascontiguousarray(np.asarray(L0_attn_Wq, np.float32)),
        "wk": np.ascontiguousarray(np.asarray(L0_attn_Wk, np.float32)),
        "wv": np.ascontiguousarray(np.asarray(L0_attn_Wv, np.float32)),
        "wo": np.ascontiguousarray(np.asarray(L0_attn_Wo, np.float32)),
        "bo": np.ascontiguousarray(np.asarray(L0_attn_bo, np.float32).reshape(1, D)),
        "wr": np.ascontiguousarray(np.asarray(L0_ffn_W_r, np.float32)),
        "wk2": np.ascontiguousarray(np.asarray(L0_ffn_W_k, np.float32)),
        "wv2": np.ascontiguousarray(np.asarray(L0_ffn_W_v, np.float32)),
        "mur": np.ascontiguousarray(np.asarray(L0_ffn_mu_r, np.float32).reshape(8, 128).T),
        "muk": np.ascontiguousarray(np.asarray(L0_ffn_mu_k, np.float32).reshape(8, 128).T),
    }
    if _NC_CACHE is None:
        _NC_CACHE = build_nc()
    nc = _NC_CACHE
    in_maps = [dict(w, x=np.ascontiguousarray(x[i])) for i in range(B)]
    res = run_bass_kernel_spmd(nc, in_maps, core_ids=list(range(B)))
    out = np.stack([np.asarray(res.results[i]["y"]) for i in range(B)], axis=0)
    return out.astype(np.float32)
